# version 13
# baseline (speedup 1.0000x reference)
import numpy as np
import ml_dtypes
import concourse.bass as bass
import concourse.mybir as mybir
from concourse.bass_utils import run_bass_kernel_spmd

F32 = mybir.dt.float32
BF16 = mybir.dt.bfloat16
I8 = mybir.dt.int8
AF = mybir.ActivationFunctionType
AX = mybir.AxisListType
ALU = mybir.AluOpType

KB = 1024
NTB = 16
NH = 16
DH = 128
NT = 2048
NC = 2048
ITOT = 8192


class Sched:

    ENGS = ("T", "V", "S", "Y", "G")
    LANES = ("T", "V", "S", "y", "g",
             "Yi", "Yx", "Ys", "Yo", "Gw", "Gg", "Gu", "Gd")

    def __init__(self):
        self.ops = {e: [] for e in self.ENGS}
        self.cnt = {e: 0 for e in self.LANES}

    def add(self, eng, fn, inc=1, waits=(), lane=None):
        lane = lane or eng
        ws = tuple((e, v) for (e, v) in waits if v > 0)
        self.ops[eng].append((fn, inc, ws, lane))
        self.cnt[lane] += inc
        return self.cnt[lane]

    def mark(self, lane):
        return self.cnt[lane]

    def barrier(self):
        marks = {e: self.cnt[e] for e in self.LANES}
        for e in self.ENGS:
            ws = [(o, marks[o]) for o in self.LANES]
            lane = {"Y": "y", "G": "g"}.get(e, e)
            self.add(e, None, inc=1, waits=ws, lane=lane)
        return {e: self.cnt[e] for e in self.LANES}


def build(debug=False):
    nc = bass.Bass("TRN2", target_bir_lowering=False)

    x_in = nc.dram_tensor("x_in", [NT, NC], F32, kind="ExternalInput")
    wt_in = nc.dram_tensor("wt_in", [NC, NC], F32, kind="ExternalInput")
    ws_in = nc.dram_tensor("ws_in", [NC, NC], F32, kind="ExternalInput")
    g1_in = nc.dram_tensor("g1_in", [128, 1], F32, kind="ExternalInput")
    g2_in = nc.dram_tensor("g2_in", [128, 1], F32, kind="ExternalInput")
    g3_in = nc.dram_tensor("g3_in", [128, 1], F32, kind="ExternalInput")
    wg_in = nc.dram_tensor("wg_in", [16, 128, 16, 512], I8, kind="ExternalInput")
    wu_in = nc.dram_tensor("wu_in", [16, 128, 16, 512], I8, kind="ExternalInput")
    wd_in = nc.dram_tensor("wd_in", [ITOT, NC], I8, kind="ExternalInput")
    sg_in = nc.dram_tensor("sg_in", [128, 64], F32, kind="ExternalInput")
    su_in = nc.dram_tensor("su_in", [128, 64], F32, kind="ExternalInput")
    sd_in = nc.dram_tensor("sd_in", [1, NC], F32, kind="ExternalInput")
    id_in = nc.dram_tensor("id_in", [128, 128], BF16, kind="ExternalInput")
    out_hbm = nc.dram_tensor("out", [1024, NC], F32, kind="ExternalOutput")
    s_dram = nc.dram_tensor("s_dram", [1024, NC], BF16, kind="Internal")
    dbg = {}
    if debug:
        for nm, shape, dt in [
                ("d_xnT", [128, 16, 1024], BF16), ("d_rn", [128, 16, 2048], BF16),
                ("d_nM", [128, 16], F32), ("d_S", [128, 16], F32),
                ("d_Ei", [128, 16, 1024], BF16), ("d_fbc", [128, 1024], F32),
                ("d_y", [128, 16, 1024], BF16), ("d_s", [1024, NC], BF16),
                ("d_fuse", [128, 64, 512], BF16), ("d_sdbc", [128, 2048], F32),
                ("d_dn", [128, 2048], F32), ("d_x0", [128, 2048], F32),
                ("d_lnst", [128, 8], F32), ("d_xn0", [128, 2048], BF16)]:
            dbg[nm] = nc.dram_tensor(nm, shape, dt, kind="ExternalOutput")

    base = (nc.sbuf_base + 31) & ~31

    def sb(name, shape, dtype, off):
        return nc.alloc_sbuf_tensor_at(name, shape, dtype, offset=base + off)

    xnT_lo = sb("xnT_lo", [128, 16, 1024], BF16, 0)
    whT = sb("whT", [128, 16, 2048], BF16, 32 * KB)
    Ei = sb("Ei", [128, 16, 1024], BF16, 32 * KB)
    ws_hi = sb("ws_hi", [128, 8, 2048], BF16, 32 * KB)
    y_st = sb("y_st", [128, 16, 1024], BF16, 64 * KB)
    fuse = sb("fuse", [128, 64, 512], BF16, 32 * KB)
    rn = sb("rn", [128, 16, 2048], BF16, 96 * KB)
    wgb = [sb(f"wgb{i}", [128, 16, 512], BF16, 96 * KB + i * 16 * KB) for i in range(2)]
    wub = [sb(f"wub{i}", [128, 16, 512], BF16, 128 * KB + i * 16 * KB) for i in range(2)]
    scrA = sb("scrA", [128, 2048], F32, 112 * KB)
    s_sb = [sb(f"s_sb{i}", [128, 2048], BF16, 120 * KB + i * 4 * KB) for i in range(2)]
    wdb = [sb(f"wdb{i}", [128, 2, 2048], BF16, 144 * KB + i * 8 * KB) for i in range(2)]
    guscr = [sb(f"guscr_{i}", [128, 512], F32, 160 * KB + 2 * KB * i) for i in range(2)]
    guscr2 = [sb(f"guscr2_{i}", [128, 512], F32, 164 * KB + 2 * KB * i) for i in range(2)]
    x_t = [sb(f"x_t{i}", [128, 2048], F32, o)
           for i, o in enumerate((160 * KB, 168 * KB, 192 * KB))]
    xn_t = [sb(f"xn_t{i}", [128, 2048], BF16, 176 * KB + i * 4 * KB) for i in range(2)]
    xnT_t = [sb(f"xnT_t{i}", [128, 16, 128], BF16, 184 * KB + i * 4 * KB) for i in range(2)]
    rnT = sb("rnT", [128, 2048], BF16, 160 * KB)
    rnTq = sb("rnTq", [128, 2048], BF16, 164 * KB)
    vv1 = sb("vv1", [128, 16, 128], BF16, 168 * KB)
    vv2 = sb("vv2", [128, 16, 128], BF16, 172 * KB)
    escr = sb("escr", [128, 1024], BF16, 176 * KB)
    frow = sb("frow", [1, 1024], BF16, 178 * KB)
    scY1 = sb("scY1", [128, 1024], F32, 180 * KB)
    scY2 = sb("scY2", [128, 1024], F32, 184 * KB)
    f_bc = sb("f_bc", [128, 1024], F32, 188 * KB)
    ws_lo = sb("ws_lo", [128, 8, 2048], BF16, 160 * KB)
    xr_t = [sb(f"xr_t{i}", [128, 2048], F32, 160 * KB + i * 8 * KB) for i in range(2)]
    outs_t = [sb(f"outs_t{i}", [128, 2048], F32, 176 * KB + i * 8 * KB) for i in range(2)]
    xsq = sb("xsq", [128, 2048], BF16, 203 * KB)
    s_stage = [sb(f"s_stage{i}", [128, 2048], BF16, 192 * KB + i * 4 * KB) for i in range(2)]
    sd_bc = sb("sd_bc", [128, 2048], F32, 192 * KB)
    sml = 200 * KB
    consts = sb("consts", [128, 8], F32, sml)
    gv = sb("gv", [128, 4], F32, sml + 64)
    lnst = sb("lnst", [128, 8], F32, sml + 128)
    nM = sb("nM", [128, 16], F32, sml + 192)
    S1 = sb("S1", [128, 16], F32, sml + 256)
    S2 = sb("S2", [128, 16], F32, sml + 320)
    Sc = sb("Sc", [128, 16], F32, sml + 384)
    rS = sb("rS", [128, 16], F32, sml + 448)
    eMf = sb("eMf", [128, 16], F32, sml + 512)
    eNg = sb("eNg", [128, 16], F32, sml + 576)
    eMb = sb("eMb", [128, 16], BF16, sml + 640)
    cs_sb = sb("cs_sb", [128, 8], F32, sml + 704)
    fcol = sb("fcol", [128, 8], F32, sml + 768)
    fbf = sb("fbf", [128, 8], BF16, sml + 832)
    rmsS = sb("rmsS", [128, 16], F32, sml + 896)
    rsv = sb("rsv", [128, 16], F32, sml + 960)
    sgv = sb("sgv", [128, 64], F32, sml + 1024)
    suv = sb("suv", [128, 64], F32, sml + 1280)
    ident = sb("ident", [128, 128], BF16, sml + 1536)
    ones1 = sb("ones1", [1, 128], BF16, sml + 1792)
    ones1f = sb("ones1f", [1, 128], F32, sml + 2048)
    nMu = sb("nMu", [128, 16], F32, sml + 2688)
    sd_row = sb("sd_row", [1, 2048], F32, 184 * KB)

    C_EPSLN, C_EPSRMS, C_NHALF, C_I128, C_I2048, C_NEG1 = range(6)

    sch = Sched()
    A = sch.add
    M = sch.mark

    A("G", lambda g: g.memset(consts[:, C_EPSLN:C_EPSLN + 1], 1e-5), 1, lane="g")
    A("G", lambda g: g.memset(consts[:, C_EPSRMS:C_EPSRMS + 1], 1.1920928955078125e-07), 1, lane="g")
    A("G", lambda g: g.memset(consts[:, C_NHALF:C_NHALF + 1], -0.5), 1, lane="g")
    A("G", lambda g: g.memset(consts[:, C_I128:C_I128 + 1], 1.0 / 128.0), 1, lane="g")
    A("G", lambda g: g.memset(consts[:, C_I2048:C_I2048 + 1], 1.0 / 2048.0), 1, lane="g")
    A("G", lambda g: g.memset(consts[:, C_NEG1:C_NEG1 + 1], -1.0), 1, lane="g")
    A("G", lambda g: g.memset(ones1[:, :], 1.0), 1, lane="g")
    A("G", lambda g: g.memset(ones1f[:, :], 1.0), 1, lane="g")
    A("G", lambda g: g.memset(lnst[:, :], 0.0), 1, lane="g")
    A("G", lambda g: g.memset(nM[:, :], 0.0), 1, lane="g")
    A("Y", lambda y: y.dma_start(out=gv[:, 0:1], in_=g1_in.ap()), 16, lane="Yi")
    A("Y", lambda y: y.dma_start(out=gv[:, 1:2], in_=g2_in.ap()), 16, lane="Yi")
    A("Y", lambda y: y.dma_start(out=gv[:, 3:4], in_=g3_in.ap()), 16, lane="Yi")
    A("Y", lambda y: y.dma_start(out=sgv[:, :], in_=sg_in.ap()), 16, lane="Yi")
    A("Y", lambda y: y.dma_start(out=suv[:, :], in_=su_in.ap()), 16, lane="Yi")
    A("Y", lambda y: y.dma_start(out=ident[:, :], in_=id_in.ap()), 16, lane="Yi")
    yg = M("Yi")
    A("V", lambda v: v.tensor_mul(gv[:, 2:3], gv[:, 0:1], gv[:, 1:2]), 1, [("Yi", yg)])
    A("V", lambda v: v.tensor_scalar_mul(gv[:, 2:3], gv[:, 2:3], 0.08838834764831843), 1)
    bar = sch.barrier()

    whT_marks = [None] * 16
    for kk in range(8):
        ln = "Gw" if kk % 2 == 0 else "Gu"
        A("G", lambda g, kk=kk: g.dma_start(
            out=whT[:, 2 * kk:2 * kk + 2, :],
            in_=wt_in[256 * kk:256 * (kk + 1), :].rearrange(
                "(k p) c -> p k c", k=2)), 16, lane=ln)
        whT_marks[2 * kk] = whT_marks[2 * kk + 1] = (ln, M(ln))

    ln_xn_mark = [0, 0, 0]
    tr_done = [0, 0]
    trv_done = [0, 0, 0, 0]
    pw_user = [0, 0]
    xload = [0, 0, 0]
    xnTt_user = [0, 0]
    rmsq_user = [0]
    trv_all_t = [0] * NTB

    def emit_ln_tr(tb):
        bi, bj = tb % 3, tb % 2
        A("Y", lambda y: y.dma_start(
            out=x_t[bi][:, :], in_=x_in[128 * tb:128 * (tb + 1), :]), 16,
          [("S", ln_xn_mark[bi])], lane="Yx")
        xload[bi] = M("Yx")
        A("S", lambda s: s.activation(
            xsq[:, :], x_t[bi][:, :], AF.Square, bias=0.0, scale=1.0,
            accum_out=lnst[:, 0:1]), 1, [("Yx", xload[bi])])
        A("S", lambda s: s.activation(
            xsq[:, :], x_t[bi][:, :], AF.Identity, bias=0.0, scale=1.0,
            accum_out=lnst[:, 1:2]), 1)
        ssum = M("S")
        A("V", lambda v: v.tensor_scalar_mul(lnst[:, 2:3], lnst[:, 1:2], 1.0 / 2048.0),
          1, [("S", ssum)])
        A("V", lambda v: v.tensor_mul(lnst[:, 3:4], lnst[:, 2:3], lnst[:, 2:3]), 1)
        A("V", lambda v: v.tensor_scalar(lnst[:, 4:5], lnst[:, 0:1], 1.0 / 2048.0,
                                         lnst[:, 3:4], ALU.mult, ALU.subtract), 1)
        vstat = M("V")
        A("S", lambda s: s.activation(lnst[:, 5:6], lnst[:, 4:5], AF.Ln,
                                      bias=consts[:, C_EPSLN:C_EPSLN + 1], scale=1.0),
          1, [("V", vstat)])
        A("S", lambda s: s.activation(lnst[:, 5:6], lnst[:, 5:6], AF.Exp,
                                      bias=0.0, scale=consts[:, C_NHALF:C_NHALF + 1]), 1)
        srstd = M("S")
        A("V", lambda v: v.tensor_scalar(lnst[:, 6:7], lnst[:, 2:3], lnst[:, 5:6],
                                         -1.0, ALU.mult, ALU.mult),
          1, [("S", srstd)])
        vnmr = M("V")
        A("S", lambda s: s.activation(
            xn_t[bj][:, :], x_t[bi][:, :], AF.Identity,
            bias=lnst[:, 6:7], scale=lnst[:, 5:6]), 1,
          [("V", vnmr), ("T", tr_done[bj])])
        ln_xn_mark[bi] = M("S")

    def emit_tr(tb):
        bi, bj = tb % 3, tb % 2
        for grp in range(4):
            bank = 4 + (grp % 2)
            def tfn(t, tb=tb, bj=bj, grp=grp, bank=bank):
                last = None
                for kk in range(4):
                    k = grp * 4 + kk
                    last = t.matmul(
                        PS[:, bank * 512 + kk * 128: bank * 512 + (kk + 1) * 128],
                        xn_t[bj][:, 128 * k:128 * (k + 1)], ident[:, :],
                        start=True, stop=True)
                return last
            A("T", tfn, 1, [("S", ln_xn_mark[bi]), ("S", trv_done[2 * (grp % 2)])])
            tg = M("T")
            if tb < 8:
                def vfn(v, tb=tb, grp=grp, bank=bank):
                    return v.copy(
                        xnT_lo[:, 4 * grp:4 * (grp + 1), 128 * tb:128 * (tb + 1)],
                        PS[:, bank * 512:(bank + 1) * 512])
            else:
                def vfn(v, bj=bj, grp=grp, bank=bank):
                    return v.copy(
                        xnT_t[bj][:, 4 * grp:4 * (grp + 1), :],
                        PS[:, bank * 512:(bank + 1) * 512])
            A("S", vfn, 1, [("T", tg), ("T", xnTt_user[bj] if tb >= 8 else 0)])
            trv_done[2 * (grp % 2)] = M("S")
        tr_done[bj] = M("T")
        trv_all_t[tb] = M("S")

    def emit_wproj(tb):
        for fh in range(2):
            slot = (2 * tb + fh) % 2
            pofs = slot * 1024
            for k in range(16):
                def wfn(t, tb=tb, fh=fh, pofs=pofs, k=k):
                    lhsT = (xnT_lo[:, k, 128 * tb:128 * (tb + 1)] if tb < 8
                            else xnT_t[tb % 2][:, k, :])
                    last = None
                    for n in range(2):
                        last = t.matmul(
                            PS[:, pofs + 512 * n: pofs + 512 * (n + 1)],
                            lhsT,
                            whT[:, k, 1024 * fh + 512 * n: 1024 * fh + 512 * (n + 1)],
                            start=(k == 0), stop=(k == 15))
                    return last
                kw = [("S", trv_all_t[tb]), ("V", pw_user[slot])] if k == 0 else []
                A("T", wfn, 1, kw + [whT_marks[k]])
            twp = M("T")
            if tb >= 8 and fh == 1:
                xnTt_user[tb % 2] = twp
            A("S", lambda s, pofs=pofs: s.activation(
                bass.AP(PS, 3072, [[4096, 128], [1, 1024]]),
                bass.AP(PS, pofs, [[4096, 128], [1, 1024]]),
                AF.Square, bias=0.0, scale=1.0), 1,
              [("T", twp), ("V", rmsq_user[0])])
            swsq = M("S")
            A("V", lambda v, fh=fh: v.reduce_sum(
                rmsS[:, 8 * fh:8 * (fh + 1)],
                bass.AP(PS, 3072, [[4096, 128], [128, 8], [1, 128]]),
                axis=AX.X), 1, [("S", swsq)])
            vred = M("V")
            rmsq_user[0] = vred
            A("S", lambda s, fh=fh: s.activation(
                rsv[:, 8 * fh:8 * (fh + 1)], rmsS[:, 8 * fh:8 * (fh + 1)], AF.Ln,
                bias=consts[:, C_EPSRMS:C_EPSRMS + 1],
                scale=consts[:, C_I128:C_I128 + 1]), 1, [("V", vred)])
            A("S", lambda s, fh=fh: s.activation(
                rsv[:, 8 * fh:8 * (fh + 1)], rsv[:, 8 * fh:8 * (fh + 1)], AF.Exp,
                bias=0.0, scale=consts[:, C_NHALF:C_NHALF + 1]), 1)
            srs = M("S")
            def rnfn(v, tb=tb, fh=fh, pofs=pofs):
                rs_b = bass.AP(rsv, 8 * fh, [[16, 128], [1, 8], [0, 128]])
                srcp = bass.AP(PS, pofs, [[4096, 128], [128, 8], [1, 128]])
                return v.tensor_tensor(
                    bass.AP(rn, tb * 2048 + fh * 1024, [[32768, 128], [128, 8], [1, 128]]),
                    srcp, rs_b, ALU.mult)
            A("V", rnfn, 1, [("S", srs)])
            pw_user[slot] = M("V")

    for tb in range(NTB + 1):
        if tb < NTB:
            emit_ln_tr(tb)
        if tb >= 1:
            emit_wproj(tb - 1)
        if tb < NTB:
            emit_tr(tb)

    bar = sch.barrier()
    if debug:
        A("Y", lambda y: y.dma_start(out=dbg["d_xnT"].ap(), in_=xnT_lo[:, :, :]), 16, lane="Yi")
        A("Y", lambda y: y.dma_start(out=dbg["d_rn"].ap(), in_=rn[:, :, :]), 16, lane="Yi")
        sch.barrier()

    head_tail_V = 0
    head_tail_S = 0
    prev_csV = 0
    prev_fbcV = 0
    prev_y2T = 0
    prev_exp15 = 0
    prev_vm15 = 0
    prev_scY1 = 0
    prev_scY2 = 0
    trv = [0, 0]
    for h in range(NH):
        for grp in range(4):
            bank = 4 if grp % 2 == 0 else 7
            def tfn(t, h=h, grp=grp, bank=bank):
                last = None
                for kk in range(4):
                    jc = grp * 4 + kk
                    last = t.matmul(
                        PS[:, bank * 512 + kk * 128: bank * 512 + (kk + 1) * 128],
                        rn[:, jc, 128 * h:128 * (h + 1)], ident[:, :],
                        start=True, stop=True)
                return last
            ws_ = [("S", trv[grp % 2])]
            if grp % 2 == 1:
                ws_.append(("V", prev_csV))
            else:
                ws_ += [("S", prev_exp15), ("V", prev_vm15)]
            A("T", tfn, 1, ws_)
            tg = M("T")
            A("S", lambda s, grp=grp, bank=bank: s.copy(
                rnT[:, 512 * grp:512 * (grp + 1)],
                PS[:, bank * 512:(bank + 1) * 512]), 1, [("T", tg)])
            trv[grp % 2] = M("S")
        A("V", lambda v: v.tensor_scalar_mul(rnTq[:, :], rnT[:, :], gv[:, 2:3]),
          1, [("S", trv[0]), ("S", trv[1])])
        A("V", lambda v: v.tensor_scalar_min(nMu[:, :], nM[:, :], 0.0), 1)
        vnmu = M("V")
        vq = M("V")
        exp_done = [0] * 16
        vm_marks = [0] * 16
        for rb in range(16):
            if rb < 8:
                pofs = (rb % 2) * 1024
                width = 1024
                war = ([("S", exp_done[rb - 2]), ("V", vm_marks[rb - 2])]
                       if rb >= 2 else [])
                if rb == 0:
                    war += [("V", prev_scY1)]
                elif rb == 1:
                    war += [("V", prev_scY2)]
            else:
                pofs = ((rb - 8) % 2) * 2048
                width = 2048
                if rb == 8:
                    war = [("S", exp_done[6]), ("S", exp_done[7]),
                           ("V", vm_marks[6]), ("V", vm_marks[7])]
                elif rb == 9:
                    war = [("S", exp_done[7]), ("V", prev_fbcV), ("T", prev_y2T),
                           ("S", trv[1]), ("V", vm_marks[7])]
                else:
                    war = [("S", exp_done[rb - 2]), ("V", vm_marks[rb - 2])]
            def afn(t, h=h, rb=rb, pofs=pofs, width=width):
                last = None
                for n in range(width // 512):
                    last = t.matmul(
                        PS[:, pofs + 512 * n: pofs + 512 * (n + 1)],
                        rnTq[:, 128 * rb:128 * (rb + 1)],
                        rnT[:, 512 * n:512 * (n + 1)],
                        start=True, stop=True)
                return last
            A("T", afn, 1, [("V", vq)] + war)
            ta = M("T")
            A("S", lambda s, rb=rb, pofs=pofs: s.activation(
                Ei[:, rb, :], PS[:, pofs:pofs + 1024], AF.Exp,
                bias=nMu[:, rb:rb + 1], scale=1.0,
                accum_out=S1[:, rb:rb + 1]), 1,
              [("T", ta)] + ([("T", prev_y2T), ("V", vnmu)] if rb == 0 else []))
            if rb >= 8:
                A("S", lambda s, rb=rb, pofs=pofs: s.activation(
                    escr[:, :], PS[:, pofs + 1024:pofs + 2048], AF.Exp,
                    bias=nMu[:, rb:rb + 1], scale=1.0,
                    accum_out=S2[:, rb:rb + 1]), 1)
            exp_done[rb] = M("S")
            A("V", lambda v, rb=rb, pofs=pofs: v.reduce_max(
                nM[:, rb:rb + 1], PS[:, pofs:pofs + 512], axis=AX.X,
                negate=True), 1, [("S", exp_done[rb])])
            vm_marks[rb] = M("V")
            if rb == 15:
                vm15 = vm_marks[rb]
        sexp_all = M("S")
        A("S", lambda s: s.activation(eMb[:, :], nMu[:, :], AF.Exp, bias=0.0,
                                      scale=consts[:, C_NEG1:C_NEG1 + 1]), 1)
        semb = M("S")
        A("S", lambda s: s.activation(eMf[:, :], nMu[:, :], AF.Exp, bias=0.0,
                                      scale=consts[:, C_NEG1:C_NEG1 + 1]), 1)
        semf = M("S")
        A("S", lambda s: s.activation(eNg[:, :], nMu[:, :], AF.Exp, bias=0.0, scale=1.0), 1)
        seng = M("S")
        def csfn(t, h=h):
            last = None
            for jb in range(8):
                for rb in range(8, 16):
                    last = t.matmul(
                        PS[:, 3584 + jb:3584 + jb + 1],
                        Ei[:, rb, 128 * jb:128 * (jb + 1)],
                        eMb[:, rb:rb + 1],
                        start=(rb == 8), stop=(rb == 15))
            return last
        A("T", csfn, 1, [("S", semb)])
        tcs = M("T")
        def vv2fn(v, h=h):
            last = None
            for rb in range(16):
                last = v.tensor_scalar_mul(vv2[:, rb, :],
                                           rn[:, rb, 128 * h:128 * (h + 1)],
                                           eMf[:, rb:rb + 1])
            return last
        A("V", vv2fn, 1, [("S", semf)])
        vvv2 = M("V")
        def y1fn(t, h=h):
            last = None
            for rb in range(16):
                for n in range(2):
                    last = t.matmul(PS[:, 512 * n:512 * (n + 1)],
                                    vv2[:, rb, :], Ei[:, rb, 512 * n:512 * (n + 1)],
                                    start=(rb == 0), stop=(rb == 15))
            return last
        A("T", y1fn, 1, [("V", vvv2), ("S", sexp_all)])
        ty1 = M("T")
        A("V", lambda v: v.tensor_copy(cs_sb[:, :], PS[:, 3584:3592]), 1,
          [("T", tcs), ("S", seng)])
        csV = M("V")
        A("V", lambda v: v.tensor_mul(cs_sb[:, :], cs_sb[:, :], eNg[:, 0:8]), 1)
        A("V", lambda v: v.tensor_add(Sc[:, 0:8], S1[:, 0:8], cs_sb[:, :]), 1)
        A("V", lambda v: v.tensor_add(Sc[:, 8:16], S1[:, 8:16], S2[:, 8:16]), 1)
        A("V", lambda v: v.reciprocal(rS[:, :], Sc[:, :]), 1)
        A("V", lambda v: v.tensor_mul(fcol[:, :], eNg[:, 0:8], rS[:, 0:8]), 1)
        A("V", lambda v: v.tensor_copy(fbf[:, :], fcol[:, :]), 1)
        vf = M("V")
        def vv1fn(v, h=h):
            last = None
            for rb in range(16):
                last = v.tensor_scalar_mul(vv1[:, rb, :],
                                           rn[:, rb, 128 * h:128 * (h + 1)],
                                           rS[:, rb:rb + 1])
            return last
        A("V", vv1fn, 1)
        vvv1 = M("V")
        def ftfn(t, h=h):
            last = None
            for jb in range(8):
                last = t.matmul(
                    PS[0:1, 2560 + 128 * jb:2560 + 128 * (jb + 1)],
                    fbf[:, jb:jb + 1], ident[:, :], start=True, stop=True)
            return last
        A("T", ftfn, 1, [("V", vf)])
        tft = M("T")
        A("V", lambda v: v.tensor_copy(frow[:, :], PS[0:1, 2560:3584]), 1, [("T", tft)])
        vfrow = M("V")
        def fbcfn(t, h=h):
            t.matmul(PS[:, 2560:3072], ones1[0:1, :], frow[0:1, 0:512],
                     start=True, stop=True)
            return t.matmul(PS[:, 3072:3584], ones1[0:1, :], frow[0:1, 512:1024],
                            start=True, stop=True)
        A("T", fbcfn, 1, [("V", vfrow)])
        tfbc = M("T")
        A("V", lambda v: v.tensor_copy(f_bc[:, :], PS[:, 2560:3584]), 1, [("T", tfbc)])
        prev_fbcV = M("V")
        A("V", lambda v: v.tensor_tensor(scY1[:, :], PS[:, 0:1024], f_bc[:, :],
                                         ALU.mult), 1, [("T", ty1)])
        prev_scY1 = M("V")
        def y2fn(t, h=h):
            last = None
            for rb in range(16):
                for n in range(2):
                    last = t.matmul(PS[:, 1024 + 512 * n:1024 + 512 * (n + 1)],
                                    vv1[:, rb, :], Ei[:, rb, 512 * n:512 * (n + 1)],
                                    start=(rb == 0), stop=(rb == 15))
            return last
        A("T", y2fn, 1, [("V", vvv1)])
        prev_y2T = M("T")
        A("V", lambda v: v.tensor_tensor(scY2[:, :], scY1[:, :], PS[:, 1024:2048],
                                         ALU.add), 1, [("T", prev_y2T)])
        prev_scY2 = M("V")
        A("V", lambda v, h=h: v.tensor_scalar_mul(y_st[:, h, :], scY2[:, :],
                                                  gv[:, 3:4]), 1)
        head_tail_V = M("V")
        head_tail_S = M("S")
        prev_csV = csV
        prev_exp15 = exp_done[15]
        prev_vm15 = vm15
        if debug and h == 0:
            sch.barrier()
            A("Y", lambda y: y.dma_start(out=dbg["d_nM"].ap(), in_=nM[:, :]), 16, lane="Yi")
            A("Y", lambda y: y.dma_start(out=dbg["d_S"].ap(), in_=Sc[:, :]), 16, lane="Yi")
            A("Y", lambda y: y.dma_start(out=dbg["d_Ei"].ap(), in_=Ei[:, :, :]), 16, lane="Yi")
            A("Y", lambda y: y.dma_start(out=dbg["d_fbc"].ap(), in_=f_bc[:, :]), 16, lane="Yi")
            sch.barrier()

    bar = sch.barrier()
    if debug:
        A("Y", lambda y: y.dma_start(out=dbg["d_y"].ap(), in_=y_st[:, :, :]), 16, lane="Yi")
        sch.barrier()

    ws_marks = [None] * 16
    for kk in range(8):
        dst = ws_lo if kk < 4 else ws_hi
        ko = 2 * kk if kk < 4 else 2 * (kk - 4)
        ln = "Gw" if kk % 2 == 0 else "Gu"
        A("G", lambda g, kk=kk, dst=dst, ko=ko: g.dma_start(
            out=dst[:, ko:ko + 2, :],
            in_=ws_in[256 * kk:256 * (kk + 1), :].rearrange(
                "(k p) c -> p k c", k=2)), 16, lane=ln)
        ws_marks[2 * kk] = ws_marks[2 * kk + 1] = (ln, M(ln))
    sproj_user = [0, 0]
    s_stage_user = [0, 0]
    for tb in range(8):
        slot = tb % 2
        pofs = slot * 2048
        for fc in range(16):
            def spfn(t, tb=tb, pofs=pofs, fc=fc):
                lhsT = y_st[:, fc, 128 * tb:128 * (tb + 1)]
                rhs_t = ws_lo if fc < 8 else ws_hi
                last = None
                for n in range(4):
                    last = t.matmul(
                        PS[:, pofs + 512 * n:pofs + 512 * (n + 1)],
                        lhsT, rhs_t[:, fc % 8, 512 * n:512 * (n + 1)],
                        start=(fc == 0), stop=(fc == 15))
                return last
            kw = [("V", sproj_user[slot])] if fc == 0 else []
            A("T", spfn, 1, kw + [ws_marks[fc]])
        tsp = M("T")
        A("V", lambda v, slot=slot, pofs=pofs: v.tensor_copy(
            s_stage[slot][:, :], PS[:, pofs:pofs + 2048]), 1,
          [("T", tsp), ("Ys", s_stage_user[slot])])
        sproj_user[slot] = M("V")
        A("Y", lambda y, tb=tb, slot=slot: y.dma_start(
            out=s_dram[128 * tb:128 * (tb + 1), :], in_=s_stage[slot][:, :]), 16,
          [("V", sproj_user[slot])], lane="Ys")
        s_stage_user[slot] = M("Ys")

    bar = sch.barrier()
    if debug:
        A("Y", lambda y: y.dma_start(out=dbg["d_s"].ap(), in_=s_dram.ap()), 16, lane="Yi")
        sch.barrier()

    A("Y", lambda y: y.dma_start(out=sd_row[:, :], in_=sd_in.ap()), 16, lane="Yi")
    sdld = M("Yi")
    def sdfn(t):
        last = None
        for n in range(4):
            last = t.matmul(PS[:, 512 * n:512 * (n + 1)], ones1f[0:1, :],
                            sd_row[0:1, 512 * n:512 * (n + 1)],
                            start=True, stop=True)
        return last
    A("T", sdfn, 1, [("Yi", sdld)])
    tsd = M("T")
    A("V", lambda v: v.tensor_copy(sd_bc[:, :], PS[:, 0:2048]), 1, [("T", tsd)])
    vsd = M("V")
    if debug:
        A("Y", lambda y: y.dma_start(out=dbg["d_sdbc"].ap(), in_=sd_bc[:, :]), 16,
          [("V", vsd)], lane="Yi")

    wload = [0, 0]
    guv_userS = [0, 0, 0, 0]
    guv_userV = [0, 0, 0, 0]
    fuse_V2 = [0, 0]
    last_fuse_V = 0
    down_V_end = 0
    down_T_end = 0
    gate_T_end = 0
    guv_user = [0, 0, 0, 0]
    down_user = [0, 0]
    xr_user = [0, 0]
    outs_user = [0, 0]
    ssb_user = [0, 0]
    wd_user = [0, 0]
    out_store_marks = []

    for sp in range(2):
        for ic in range(16):
            wbi = ic % 2
            aw = ([("V", down_V_end), ("T", down_T_end)] if ic == 1 else [])
            A("G", lambda g, ic=ic, wbi=wbi: g.dma_start(
                out=wgb[wbi][:, :, :], in_=wg_in[ic, :, :, :]), 16,
              [("T", wload[wbi])] + aw, lane="Gg")
            A("G", lambda g, ic=ic, wbi=wbi: g.dma_start(
                out=wub[wbi][:, :, :], in_=wu_in[ic, :, :, :]), 16, aw, lane="Gu")
            gm_g = M("Gg"); gm_u = M("Gu")
            last_t = 0
            for ib in range(4):
                gb = ic * 4 + ib
                pg = (gb % 4) * 1024
                pu = pg + 512
                def gufn(t, sp=sp, ic=ic, ib=ib, wbi=wbi, pg=pg, pu=pu):
                    last = None
                    for k in range(16):
                        last = t.matmul(
                            PS[:, pg:pg + 512],
                            wgb[wbi][:, k, 128 * ib:128 * (ib + 1)],
                            xnT_lo[:, k, 512 * sp:512 * (sp + 1)],
                            start=(k == 0), stop=(k == 15))
                    for k in range(16):
                        last = t.matmul(
                            PS[:, pu:pu + 512],
                            wub[wbi][:, k, 128 * ib:128 * (ib + 1)],
                            xnT_lo[:, k, 512 * sp:512 * (sp + 1)],
                            start=(k == 0), stop=(k == 15))
                    return last
                extra = ([("V", down_V_end), ("V", vsd)] if ic == 0 else [])
                A("T", gufn, 1, [("Gg", gm_g), ("Gu", gm_u), ("S", guv_userS[gb % 4]),
                                 ("V", guv_userV[gb % 4])] + extra)
                tgu = M("T")
                last_t = tgu
                col = ic * 4 + ib
                gp = gb % 2
                A("S", lambda s, col=col, pg=pg, gp=gp: s.activation(
                    guscr[gp][:, :], PS[:, pg:pg + 512], AF.Silu,
                    bias=0.0, scale=sgv[:, col:col + 1]), 1,
                  [("T", tgu), ("V", fuse_V2[gp])])
                guv_userS[gb % 4] = M("S")
                A("V", lambda v, col=col, pu=pu, gp=gp: v.tensor_scalar_mul(
                    guscr2[gp][:, :], PS[:, pu:pu + 512], suv[:, col:col + 1]),
                  1, [("T", tgu), ("S", guv_userS[gb % 4])])
                A("V", lambda v, col=col, gp=gp: v.tensor_tensor(
                    fuse[:, col, :], guscr[gp][:, :], guscr2[gp][:, :], ALU.mult), 1)
                guv_userV[gb % 4] = M("V")
                fuse_V2[gp] = M("V")
                last_fuse_V = M("V")
            wload[wbi] = last_t
        fuse_done_V = M("V")
        gate_T_end = M("T")
        if debug and sp == 0:
            sch.barrier()
            A("Y", lambda y: y.dma_start(out=dbg["d_fuse"].ap(), in_=fuse[:, :, :]), 16, lane="Yi")
            sch.barrier()
        for tp in range(2):
            for tb2 in range(2):
                tbg = sp * 4 + tp * 2 + tb2
                A("Y", lambda y, tbg=tbg, b=tb2: y.dma_start(
                    out=xr_t[b][:, :], in_=x_in[128 * tbg:128 * (tbg + 1), :]), 16,
                  [("V", xr_user[tb2]), ("V", fuse_done_V)], lane="Yx")
                A("Y", lambda y, tbg=tbg, b=tb2: y.dma_start(
                    out=s_sb[b][:, :], in_=s_dram[128 * tbg:128 * (tbg + 1), :]), 16,
                  [("V", ssb_user[tb2]), ("T", gate_T_end)], lane="Ys")
            yxr = M("Yx"); ysb = M("Ys")
            dn_t = 0
            for r in range(32):
                wbi = r % 2
                A("G", lambda g, r=r, wbi=wbi: g.dma_start(
                    out=wdb[wbi][:, :, :],
                    in_=wd_in[256 * r:256 * (r + 1), :].rearrange(
                        "(u p) c -> p u c", u=2)), 16,
                  [("T", wd_user[wbi]), ("T", gate_T_end)], lane="Gd")
                gw = M("Gd")
                def dnfn(t, sp=sp, tp=tp, r=r, wbi=wbi):
                    last = None
                    for u in range(2):
                        ic = 2 * r + u
                        for tb2 in range(2):
                            pofs = tb2 * 2048
                            for n in range(4):
                                last = t.matmul(
                                    PS[:, pofs + 512 * n:pofs + 512 * (n + 1)],
                                    fuse[:, ic, 256 * tp + 128 * tb2:256 * tp + 128 * (tb2 + 1)],
                                    wdb[wbi][:, u, 512 * n:512 * (n + 1)],
                                    start=(r == 0 and u == 0), stop=(r == 31 and u == 1))
                    return last
                A("T", dnfn, 1, [("Gd", gw), ("V", fuse_done_V)] +
                  ([("V", down_user[0]), ("V", down_user[1])] if r == 0 else []))
                dn_t = M("T")
                wd_user[wbi] = dn_t
            for tb2 in range(2):
                tbg = sp * 4 + tp * 2 + tb2
                pofs = tb2 * 2048
                A("V", lambda v, pofs=pofs: v.tensor_tensor(
                    scrA[:, :], PS[:, pofs:pofs + 2048], sd_bc[:, :], ALU.mult),
                  1, [("T", dn_t), ("V", vsd), ("Yx", yxr), ("Ys", ysb)])
                down_user[tb2] = M("V")
                if debug and sp == 0 and tp == 0 and tb2 == 0:
                    A("Y", lambda y: y.dma_start(out=dbg["d_dn"].ap(), in_=scrA[:, :]),
                      16, [("V", down_user[0])], lane="Yi")
                    dnst = M("Yi")
                    sch.ops["S"].append((None, 1, (("Yi", dnst),), "S"))
                    sch.cnt["S"] += 1
                A("S", lambda s, b=tb2: s.activation(outs_t[b][:, :], scrA[:, :],
                                                     AF.Silu, bias=0.0, scale=1.0), 1,
                  [("V", down_user[tb2]), ("Yo", outs_user[tb2])])
                sg_m = M("S")
                A("V", lambda v, b=tb2: v.tensor_tensor(
                    scrA[:, :], outs_t[b][:, :], s_sb[b][:, :], ALU.mult), 1,
                  [("S", sg_m)])
                ssb_user[tb2] = M("V")
                A("V", lambda v, b=tb2: v.tensor_tensor(
                    outs_t[b][:, :], scrA[:, :], xr_t[b][:, :], ALU.add), 1)
                xr_user[tb2] = M("V")
                vout = M("V")
                A("Y", lambda y, tbg=tbg, b=tb2: y.dma_start(
                    out=out_hbm[128 * tbg:128 * (tbg + 1), :], in_=outs_t[b][:, :]),
                  16, [("V", vout)], lane="Yo")
                outs_user[tb2] = M("Yo")
                out_store_marks.append(M("Yo"))
            down_V_end = M("V")
            down_T_end = M("T")

    A("Y", lambda y: y.nop(), 1,
      [(ln, M(ln)) for ln in Sched.LANES], lane="y")

    from contextlib import ExitStack
    with ExitStack() as ctx:
        e = ctx.enter_context
        global PS
        psg = e(nc.psum_tensor("PS", [128, 4096], F32))
        PS_set(psg)
        sems = {ln: e(nc.semaphore("sem" + ln + ("e" if ln.islower() else "")))
                for ln in Sched.LANES}
        block = e(nc.Block())

        def emit(eng_name, eng):
            last_wait = {k: 0 for k in Sched.LANES}
            done = {k: 0 for k in Sched.LANES}
            for (fn, inc, waits, lane) in sch.ops[eng_name]:
                ws = list(waits)
                if lane != "T" and done[lane] > last_wait[lane]:
                    ws.append((lane, done[lane]))
                for (we, wv) in ws:
                    if wv > last_wait[we]:
                        eng.wait_ge(sems[we], wv)
                        last_wait[we] = wv
                if fn is None:
                    inst = eng.nop()
                else:
                    inst = fn(eng)
                if inc:
                    inst.then_inc(sems[lane], inc)
                done[lane] += inc

        block.tensor(lambda t: emit("T", t))
        block.vector(lambda v: emit("V", v))
        block.scalar(lambda s: emit("S", s))
        block.sync(lambda y: emit("Y", y))
        block.gpsimd(lambda g: emit("G", g))
    return nc


PS = None


def PS_set(handle):
    global PS
    PS = handle


_NC_CACHE = {}


def _get_nc():
    if "nc" not in _NC_CACHE:
        _NC_CACHE["nc"] = build()
    return _NC_CACHE["nc"]


def kernel(L0_attn_x, L0_attn_W, L0_attn_g1, L0_attn_g2, L0_attn_g3,
           L0_ffn_w_gate, L0_ffn_w_up, L0_ffn_w_down,
           L0_ffn_s_gate, L0_ffn_s_up, L0_ffn_s_down):
    x = np.ascontiguousarray(np.asarray(L0_attn_x, dtype=np.float32))
    W = np.ascontiguousarray(np.asarray(L0_attn_W, dtype=np.float32))
    g1 = np.asarray(L0_attn_g1, np.float32).reshape(128, 1)
    g2 = np.asarray(L0_attn_g2, np.float32).reshape(128, 1)
    g3 = np.asarray(L0_attn_g3, np.float32).reshape(128, 1)
    wg = np.asarray(L0_ffn_w_gate, np.int8)
    wu = np.asarray(L0_ffn_w_up, np.int8)
    wd = np.ascontiguousarray(np.asarray(L0_ffn_w_down, np.int8))
    sg = np.asarray(L0_ffn_s_gate, np.float32)
    su = np.asarray(L0_ffn_s_up, np.float32)
    sd = np.asarray(L0_ffn_s_down, np.float32).reshape(1, 2048)

    wt = np.ascontiguousarray(W.T)
    wg4 = np.ascontiguousarray(wg.reshape(16, 128, 16, 512).transpose(2, 1, 0, 3))
    wu4 = np.ascontiguousarray(wu.reshape(16, 128, 16, 512).transpose(2, 1, 0, 3))
    sgv = np.ascontiguousarray(sg.reshape(64, 128).T)
    suv = np.ascontiguousarray(su.reshape(64, 128).T)
    ident = np.eye(128).astype(ml_dtypes.bfloat16)

    common = {
        "wt_in": wt, "ws_in": W, "g1_in": g1, "g2_in": g2, "g3_in": g3,
        "wg_in": wg4, "wu_in": wu4, "wd_in": wd, "sg_in": sgv, "su_in": suv,
        "sd_in": sd, "id_in": ident,
    }
    in_maps = []
    for c in range(8):
        b, g = c // 2, c % 2
        if g == 0:
            xc = x[b]
        else:
            xc = np.ascontiguousarray(np.concatenate([x[b][1024:], x[b][:1024]], axis=0))
        in_maps.append(dict(common, x_in=xc))

    nc = _get_nc()
    res = run_bass_kernel_spmd(nc, in_maps, core_ids=list(range(8)))
    out = np.empty((4, 2048, 2048), np.float32)
    for c in range(8):
        b, g = c // 2, c % 2
        out[b, 1024 * g:1024 * (g + 1), :] = res.results[c]["out"]
    return out


# version 14 (final)
# speedup vs baseline: 1.1748x; 1.1748x over previous
import numpy as np
import ml_dtypes
import concourse.bass as bass
import concourse.mybir as mybir
from concourse.bass_utils import run_bass_kernel_spmd

F32 = mybir.dt.float32
BF16 = mybir.dt.bfloat16
I8 = mybir.dt.int8
AF = mybir.ActivationFunctionType
AX = mybir.AxisListType
ALU = mybir.AluOpType

KB = 1024
NTB = 16
NH = 16
DH = 128
NT = 2048
NC = 2048
ITOT = 8192


class Sched:

    ENGS = ("T", "V", "S", "Y", "G")
    LANES = ("T", "V", "S", "y", "g",
             "Yi", "Yx", "Ys", "Yo", "Gw", "Gg", "Gu", "Gd")

    def __init__(self):
        self.ops = {e: [] for e in self.ENGS}
        self.cnt = {e: 0 for e in self.LANES}

    def add(self, eng, fn, inc=1, waits=(), lane=None):
        lane = lane or eng
        ws = tuple((e, v) for (e, v) in waits if v > 0)
        self.ops[eng].append((fn, inc, ws, lane))
        self.cnt[lane] += inc
        return self.cnt[lane]

    def mark(self, lane):
        return self.cnt[lane]

    def barrier(self):
        marks = {e: self.cnt[e] for e in self.LANES}
        for e in self.ENGS:
            ws = [(o, marks[o]) for o in self.LANES]
            lane = {"Y": "y", "G": "g"}.get(e, e)
            self.add(e, None, inc=1, waits=ws, lane=lane)
        return {e: self.cnt[e] for e in self.LANES}


def build(debug=False):
    nc = bass.Bass("TRN2", target_bir_lowering=False)

    x_in = nc.dram_tensor("x_in", [NT, NC], F32, kind="ExternalInput")
    wt_in = nc.dram_tensor("wt_in", [NC, NC], F32, kind="ExternalInput")
    ws_in = nc.dram_tensor("ws_in", [NC, NC], F32, kind="ExternalInput")
    g1_in = nc.dram_tensor("g1_in", [128, 1], F32, kind="ExternalInput")
    g2_in = nc.dram_tensor("g2_in", [128, 1], F32, kind="ExternalInput")
    g3_in = nc.dram_tensor("g3_in", [128, 1], F32, kind="ExternalInput")
    wg_in = nc.dram_tensor("wg_in", [16, 128, 16, 512], I8, kind="ExternalInput")
    wu_in = nc.dram_tensor("wu_in", [16, 128, 16, 512], I8, kind="ExternalInput")
    wd_in = nc.dram_tensor("wd_in", [ITOT, NC], I8, kind="ExternalInput")
    sg_in = nc.dram_tensor("sg_in", [128, 64], F32, kind="ExternalInput")
    su_in = nc.dram_tensor("su_in", [128, 64], F32, kind="ExternalInput")
    sd_in = nc.dram_tensor("sd_in", [1, NC], F32, kind="ExternalInput")
    id_in = nc.dram_tensor("id_in", [128, 128], BF16, kind="ExternalInput")
    out_hbm = nc.dram_tensor("out", [1024, NC], F32, kind="ExternalOutput")
    s_dram = nc.dram_tensor("s_dram", [1024, NC], BF16, kind="Internal")
    dbg = {}
    if debug:
        for nm, shape, dt in [
                ("d_xnT", [128, 16, 1024], BF16), ("d_rn", [128, 16, 2048], BF16),
                ("d_nM", [128, 16], F32), ("d_S", [128, 16], F32),
                ("d_Ei", [128, 16, 1024], BF16), ("d_fbc", [128, 1024], F32),
                ("d_y", [128, 16, 1024], BF16), ("d_s", [1024, NC], BF16),
                ("d_fuse", [128, 64, 512], BF16), ("d_sdbc", [128, 2048], F32),
                ("d_dn", [128, 2048], F32), ("d_x0", [128, 2048], F32),
                ("d_lnst", [128, 8], F32), ("d_xn0", [128, 2048], BF16)]:
            dbg[nm] = nc.dram_tensor(nm, shape, dt, kind="ExternalOutput")

    base = (nc.sbuf_base + 31) & ~31

    def sb(name, shape, dtype, off):
        return nc.alloc_sbuf_tensor_at(name, shape, dtype, offset=base + off)

    xnT_lo = sb("xnT_lo", [128, 16, 1024], BF16, 0)
    whT = sb("whT", [128, 16, 2048], BF16, 32 * KB)
    Ei = sb("Ei", [128, 16, 1024], BF16, 32 * KB)
    ws_hi = sb("ws_hi", [128, 8, 2048], BF16, 32 * KB)
    y_st = sb("y_st", [128, 16, 1024], BF16, 64 * KB)
    fuse = sb("fuse", [128, 64, 512], BF16, 32 * KB)
    rn = sb("rn", [128, 16, 2048], BF16, 96 * KB)
    wgb = [sb(f"wgb{i}", [128, 16, 512], BF16, 96 * KB + i * 16 * KB) for i in range(2)]
    wub = [sb(f"wub{i}", [128, 16, 512], BF16, 128 * KB + i * 16 * KB) for i in range(2)]
    scrA = sb("scrA", [128, 2048], F32, 112 * KB)
    s_sb = [sb(f"s_sb{i}", [128, 2048], BF16, 120 * KB + i * 4 * KB) for i in range(2)]
    wdb = [sb(f"wdb{i}", [128, 2, 2048], BF16, 144 * KB + i * 8 * KB) for i in range(2)]
    guscr = [sb(f"guscr_{i}", [128, 512], F32, 160 * KB + 2 * KB * i) for i in range(2)]
    guscr2 = [sb(f"guscr2_{i}", [128, 512], F32, 164 * KB + 2 * KB * i) for i in range(2)]
    x_t = [sb(f"x_t{i}", [128, 2048], F32, o)
           for i, o in enumerate((160 * KB, 168 * KB, 192 * KB))]
    xn_t = [sb(f"xn_t{i}", [128, 2048], BF16, 176 * KB + i * 4 * KB) for i in range(2)]
    xnT_t = [sb(f"xnT_t{i}", [128, 16, 128], BF16, 184 * KB + i * 4 * KB) for i in range(2)]
    rnT = sb("rnT", [128, 2048], BF16, 160 * KB)
    rnTq = sb("rnTq", [128, 2048], BF16, 164 * KB)
    vv1 = sb("vv1", [128, 16, 128], BF16, 168 * KB)
    vv2 = sb("vv2", [128, 16, 128], BF16, 172 * KB)
    escr = sb("escr", [128, 1024], BF16, 176 * KB)
    frow = sb("frow", [1, 1024], BF16, 178 * KB)
    scY1 = sb("scY1", [128, 1024], F32, 180 * KB)
    scY2 = sb("scY2", [128, 1024], F32, 184 * KB)
    f_bc = sb("f_bc", [128, 1024], F32, 188 * KB)
    ws_lo = sb("ws_lo", [128, 8, 2048], BF16, 160 * KB)
    xr_t = [sb(f"xr_t{i}", [128, 2048], F32, 160 * KB + i * 8 * KB) for i in range(2)]
    outs_t = [sb(f"outs_t{i}", [128, 2048], F32, 176 * KB + i * 8 * KB) for i in range(2)]
    xsq = sb("xsq", [128, 2048], BF16, 203 * KB)
    s_stage = [sb(f"s_stage{i}", [128, 2048], BF16, 192 * KB + i * 4 * KB) for i in range(2)]
    sd_bc = sb("sd_bc", [128, 2048], F32, 192 * KB)
    sml = 200 * KB
    consts = sb("consts", [128, 8], F32, sml)
    gv = sb("gv", [128, 4], F32, sml + 64)
    lnst = sb("lnst", [128, 8], F32, sml + 128)
    nM = sb("nM", [128, 16], F32, sml + 192)
    S1 = sb("S1", [128, 16], F32, sml + 256)
    S2 = sb("S2", [128, 16], F32, sml + 320)
    Sc = sb("Sc", [128, 16], F32, sml + 384)
    rS = sb("rS", [128, 16], F32, sml + 448)
    eMf = sb("eMf", [128, 16], F32, sml + 512)
    eNg = sb("eNg", [128, 16], F32, sml + 576)
    eMb = sb("eMb", [128, 16], BF16, sml + 640)
    cs_sb = sb("cs_sb", [128, 8], F32, sml + 704)
    fcol = sb("fcol", [128, 8], F32, sml + 768)
    fbf = sb("fbf", [128, 8], BF16, sml + 832)
    rmsS = sb("rmsS", [128, 16], F32, sml + 896)
    rsv = sb("rsv", [128, 16], F32, sml + 960)
    sgv = sb("sgv", [128, 64], F32, sml + 1024)
    suv = sb("suv", [128, 64], F32, sml + 1280)
    ident = sb("ident", [128, 128], BF16, sml + 1536)
    ones1 = sb("ones1", [1, 128], BF16, sml + 1792)
    ones1f = sb("ones1f", [1, 128], F32, sml + 2048)
    nMu = sb("nMu", [128, 16], F32, sml + 2688)
    sd_row = sb("sd_row", [1, 2048], F32, 184 * KB)

    C_EPSLN, C_EPSRMS, C_NHALF, C_I128, C_I2048, C_NEG1 = range(6)

    sch = Sched()
    A = sch.add
    M = sch.mark

    A("G", lambda g: g.memset(consts[:, C_EPSLN:C_EPSLN + 1], 1e-5), 1, lane="g")
    A("G", lambda g: g.memset(consts[:, C_EPSRMS:C_EPSRMS + 1], 1.1920928955078125e-07), 1, lane="g")
    A("G", lambda g: g.memset(consts[:, C_NHALF:C_NHALF + 1], -0.5), 1, lane="g")
    A("G", lambda g: g.memset(consts[:, C_I128:C_I128 + 1], 1.0 / 128.0), 1, lane="g")
    A("G", lambda g: g.memset(consts[:, C_I2048:C_I2048 + 1], 1.0 / 2048.0), 1, lane="g")
    A("G", lambda g: g.memset(consts[:, C_NEG1:C_NEG1 + 1], -1.0), 1, lane="g")
    A("G", lambda g: g.memset(ones1[:, :], 1.0), 1, lane="g")
    A("G", lambda g: g.memset(ones1f[:, :], 1.0), 1, lane="g")
    A("G", lambda g: g.memset(lnst[:, :], 0.0), 1, lane="g")
    A("G", lambda g: g.memset(nMu[:, :], 0.0), 1, lane="g")
    A("Y", lambda y: y.dma_start(out=gv[:, 0:1], in_=g1_in.ap()), 16, lane="Yi")
    A("Y", lambda y: y.dma_start(out=gv[:, 1:2], in_=g2_in.ap()), 16, lane="Yi")
    A("Y", lambda y: y.dma_start(out=gv[:, 3:4], in_=g3_in.ap()), 16, lane="Yi")
    A("Y", lambda y: y.dma_start(out=sgv[:, :], in_=sg_in.ap()), 16, lane="Yi")
    A("Y", lambda y: y.dma_start(out=suv[:, :], in_=su_in.ap()), 16, lane="Yi")
    A("Y", lambda y: y.dma_start(out=ident[:, :], in_=id_in.ap()), 16, lane="Yi")
    yg = M("Yi")
    A("V", lambda v: v.tensor_mul(gv[:, 2:3], gv[:, 0:1], gv[:, 1:2]), 1, [("Yi", yg)])
    A("V", lambda v: v.tensor_scalar_mul(gv[:, 2:3], gv[:, 2:3], 0.08838834764831843), 1)
    bar = sch.barrier()

    whT_marks = [None] * 16
    for kk in range(8):
        ln = "Gw" if kk % 2 == 0 else "Gu"
        A("G", lambda g, kk=kk: g.dma_start(
            out=whT[:, 2 * kk:2 * kk + 2, :],
            in_=wt_in[256 * kk:256 * (kk + 1), :].rearrange(
                "(k p) c -> p k c", k=2)), 16, lane=ln)
        whT_marks[2 * kk] = whT_marks[2 * kk + 1] = (ln, M(ln))

    ln_xn_mark = [0, 0, 0]
    tr_done = [0, 0]
    trv_done = [0, 0, 0, 0]
    pw_user = [0, 0]
    xload = [0, 0, 0]
    xnTt_user = [0, 0]
    rmsq_user = [0]
    trv_all_t = [0] * NTB

    def emit_ln_tr(tb):
        bi, bj = tb % 3, tb % 2
        A("Y", lambda y: y.dma_start(
            out=x_t[bi][:, :], in_=x_in[128 * tb:128 * (tb + 1), :]), 16,
          [("S", ln_xn_mark[bi])], lane="Yx")
        xload[bi] = M("Yx")
        A("S", lambda s: s.activation(
            xsq[:, :], x_t[bi][:, :], AF.Square, bias=0.0, scale=1.0,
            accum_out=lnst[:, 0:1]), 1, [("Yx", xload[bi])])
        A("S", lambda s: s.activation(
            xsq[:, :], x_t[bi][:, :], AF.Identity, bias=0.0, scale=1.0,
            accum_out=lnst[:, 1:2]), 1)
        ssum = M("S")
        A("V", lambda v: v.tensor_scalar_mul(lnst[:, 2:3], lnst[:, 1:2], 1.0 / 2048.0),
          1, [("S", ssum)])
        A("V", lambda v: v.tensor_mul(lnst[:, 3:4], lnst[:, 2:3], lnst[:, 2:3]), 1)
        A("V", lambda v: v.tensor_scalar(lnst[:, 4:5], lnst[:, 0:1], 1.0 / 2048.0,
                                         lnst[:, 3:4], ALU.mult, ALU.subtract), 1)
        vstat = M("V")
        A("S", lambda s: s.activation(lnst[:, 5:6], lnst[:, 4:5], AF.Ln,
                                      bias=consts[:, C_EPSLN:C_EPSLN + 1], scale=1.0),
          1, [("V", vstat)])
        A("S", lambda s: s.activation(lnst[:, 5:6], lnst[:, 5:6], AF.Exp,
                                      bias=0.0, scale=consts[:, C_NHALF:C_NHALF + 1]), 1)
        srstd = M("S")
        A("V", lambda v: v.tensor_scalar(lnst[:, 6:7], lnst[:, 2:3], lnst[:, 5:6],
                                         -1.0, ALU.mult, ALU.mult),
          1, [("S", srstd)])
        vnmr = M("V")
        A("S", lambda s: s.activation(
            xn_t[bj][:, :], x_t[bi][:, :], AF.Identity,
            bias=lnst[:, 6:7], scale=lnst[:, 5:6]), 1,
          [("V", vnmr), ("T", tr_done[bj])])
        ln_xn_mark[bi] = M("S")

    def emit_tr(tb):
        bi, bj = tb % 3, tb % 2
        for grp in range(4):
            bank = 4 + (grp % 2)
            def tfn(t, tb=tb, bj=bj, grp=grp, bank=bank):
                last = None
                for kk in range(4):
                    k = grp * 4 + kk
                    last = t.matmul(
                        PS[:, bank * 512 + kk * 128: bank * 512 + (kk + 1) * 128],
                        xn_t[bj][:, 128 * k:128 * (k + 1)], ident[:, :],
                        start=True, stop=True)
                return last
            A("T", tfn, 1, [("S", ln_xn_mark[bi]), ("S", trv_done[2 * (grp % 2)])])
            tg = M("T")
            if tb < 8:
                def vfn(v, tb=tb, grp=grp, bank=bank):
                    return v.copy(
                        xnT_lo[:, 4 * grp:4 * (grp + 1), 128 * tb:128 * (tb + 1)],
                        PS[:, bank * 512:(bank + 1) * 512])
            else:
                def vfn(v, bj=bj, grp=grp, bank=bank):
                    return v.copy(
                        xnT_t[bj][:, 4 * grp:4 * (grp + 1), :],
                        PS[:, bank * 512:(bank + 1) * 512])
            A("S", vfn, 1, [("T", tg), ("T", xnTt_user[bj] if tb >= 8 else 0)])
            trv_done[2 * (grp % 2)] = M("S")
        tr_done[bj] = M("T")
        trv_all_t[tb] = M("S")

    def emit_wproj(tb):
        for fh in range(2):
            slot = (2 * tb + fh) % 2
            pofs = slot * 1024
            for k in range(16):
                def wfn(t, tb=tb, fh=fh, pofs=pofs, k=k):
                    lhsT = (xnT_lo[:, k, 128 * tb:128 * (tb + 1)] if tb < 8
                            else xnT_t[tb % 2][:, k, :])
                    last = None
                    for n in range(2):
                        last = t.matmul(
                            PS[:, pofs + 512 * n: pofs + 512 * (n + 1)],
                            lhsT,
                            whT[:, k, 1024 * fh + 512 * n: 1024 * fh + 512 * (n + 1)],
                            start=(k == 0), stop=(k == 15))
                    return last
                kw = [("S", trv_all_t[tb]), ("V", pw_user[slot])] if k == 0 else []
                A("T", wfn, 1, kw + [whT_marks[k]])
            twp = M("T")
            if tb >= 8 and fh == 1:
                xnTt_user[tb % 2] = twp
            A("S", lambda s, pofs=pofs: s.activation(
                bass.AP(PS, 3072, [[4096, 128], [1, 1024]]),
                bass.AP(PS, pofs, [[4096, 128], [1, 1024]]),
                AF.Square, bias=0.0, scale=1.0), 1,
              [("T", twp), ("V", rmsq_user[0])])
            swsq = M("S")
            A("V", lambda v, fh=fh: v.reduce_sum(
                rmsS[:, 8 * fh:8 * (fh + 1)],
                bass.AP(PS, 3072, [[4096, 128], [128, 8], [1, 128]]),
                axis=AX.X), 1, [("S", swsq)])
            vred = M("V")
            rmsq_user[0] = vred
            A("S", lambda s, fh=fh: s.activation(
                rsv[:, 8 * fh:8 * (fh + 1)], rmsS[:, 8 * fh:8 * (fh + 1)], AF.Ln,
                bias=consts[:, C_EPSRMS:C_EPSRMS + 1],
                scale=consts[:, C_I128:C_I128 + 1]), 1, [("V", vred)])
            A("S", lambda s, fh=fh: s.activation(
                rsv[:, 8 * fh:8 * (fh + 1)], rsv[:, 8 * fh:8 * (fh + 1)], AF.Exp,
                bias=0.0, scale=consts[:, C_NHALF:C_NHALF + 1]), 1)
            srs = M("S")
            def rnfn(v, tb=tb, fh=fh, pofs=pofs):
                rs_b = bass.AP(rsv, 8 * fh, [[16, 128], [1, 8], [0, 128]])
                srcp = bass.AP(PS, pofs, [[4096, 128], [128, 8], [1, 128]])
                return v.tensor_tensor(
                    bass.AP(rn, tb * 2048 + fh * 1024, [[32768, 128], [128, 8], [1, 128]]),
                    srcp, rs_b, ALU.mult)
            A("V", rnfn, 1, [("S", srs)])
            pw_user[slot] = M("V")

    for tb in range(NTB + 1):
        if tb < NTB:
            emit_ln_tr(tb)
        if tb >= 1:
            emit_wproj(tb - 1)
        if tb < NTB:
            emit_tr(tb)

    bar = sch.barrier()
    if debug:
        A("Y", lambda y: y.dma_start(out=dbg["d_xnT"].ap(), in_=xnT_lo[:, :, :]), 16, lane="Yi")
        A("Y", lambda y: y.dma_start(out=dbg["d_rn"].ap(), in_=rn[:, :, :]), 16, lane="Yi")
        sch.barrier()

    head_tail_V = 0
    head_tail_S = 0
    prev_csV = 0
    prev_fbcV = 0
    prev_y2T = 0
    prev_exp15 = 0
    prev_s1war = 0
    prev_scY1 = 0
    prev_scY2 = 0
    trv = [0, 0]
    for h in range(NH):
        for grp in range(4):
            bank = 4 if grp % 2 == 0 else 7
            def tfn(t, h=h, grp=grp, bank=bank):
                last = None
                for kk in range(4):
                    jc = grp * 4 + kk
                    last = t.matmul(
                        PS[:, bank * 512 + kk * 128: bank * 512 + (kk + 1) * 128],
                        rn[:, jc, 128 * h:128 * (h + 1)], ident[:, :],
                        start=True, stop=True)
                return last
            ws_ = [("S", trv[grp % 2])]
            if grp % 2 == 1:
                ws_.append(("V", prev_csV))
            else:
                ws_ += [("S", prev_exp15)]
            A("T", tfn, 1, ws_)
            tg = M("T")
            A("S", lambda s, grp=grp, bank=bank: s.copy(
                rnT[:, 512 * grp:512 * (grp + 1)],
                PS[:, bank * 512:(bank + 1) * 512]), 1, [("T", tg)])
            trv[grp % 2] = M("S")
        A("V", lambda v: v.tensor_scalar_mul(rnTq[:, :], rnT[:, :], gv[:, 2:3]),
          1, [("S", trv[0]), ("S", trv[1])])
        vq = M("V")
        exp_done = [0] * 16
        for rb in range(16):
            if rb < 8:
                pofs = (rb % 2) * 1024
                width = 1024
                war = [("S", exp_done[rb - 2])] if rb >= 2 else []
                if rb == 0:
                    war += [("V", prev_scY1)]
                elif rb == 1:
                    war += [("V", prev_scY2)]
            else:
                pofs = ((rb - 8) % 2) * 2048
                width = 2048
                if rb == 8:
                    war = [("S", exp_done[6]), ("S", exp_done[7])]
                elif rb == 9:
                    war = [("S", exp_done[7]), ("V", prev_fbcV), ("T", prev_y2T),
                           ("S", trv[1])]
                else:
                    war = [("S", exp_done[rb - 2])]
            def afn(t, h=h, rb=rb, pofs=pofs, width=width):
                last = None
                for n in range(width // 512):
                    last = t.matmul(
                        PS[:, pofs + 512 * n: pofs + 512 * (n + 1)],
                        rnTq[:, 128 * rb:128 * (rb + 1)],
                        rnT[:, 512 * n:512 * (n + 1)],
                        start=True, stop=True)
                return last
            A("T", afn, 1, [("V", vq)] + war)
            ta = M("T")
            A("S", lambda s, rb=rb, pofs=pofs: s.activation(
                Ei[:, rb, :], PS[:, pofs:pofs + 1024], AF.Exp,
                bias=nMu[:, rb:rb + 1], scale=1.0,
                accum_out=S1[:, rb:rb + 1]), 1,
              [("T", ta)] + ([("T", prev_y2T), ("V", prev_s1war)] if rb == 0
                             else []))
            if rb >= 8:
                A("S", lambda s, rb=rb, pofs=pofs: s.activation(
                    escr[:, :], PS[:, pofs + 1024:pofs + 2048], AF.Exp,
                    bias=nMu[:, rb:rb + 1], scale=1.0,
                    accum_out=S2[:, rb:rb + 1]), 1)
            exp_done[rb] = M("S")
        sexp_all = M("S")
        A("S", lambda s: s.activation(eMb[:, :], nMu[:, :], AF.Exp, bias=0.0,
                                      scale=consts[:, C_NEG1:C_NEG1 + 1]), 1)
        semb = M("S")
        A("S", lambda s: s.activation(eMf[:, :], nMu[:, :], AF.Exp, bias=0.0,
                                      scale=consts[:, C_NEG1:C_NEG1 + 1]), 1)
        semf = M("S")
        A("S", lambda s: s.activation(eNg[:, :], nMu[:, :], AF.Exp, bias=0.0, scale=1.0), 1)
        seng = M("S")
        A("S", lambda s: s.activation(nM[:, :], S1[:, :], AF.Ln,
                                      bias=0.0, scale=1.0), 1)
        sln = M("S")
        A("V", lambda v: v.tensor_sub(nM[:, :], nMu[:, :], nM[:, :]), 1,
          [("S", sln)])
        A("V", lambda v: v.tensor_scalar_min(nMu[:, :], nM[:, :], 0.0), 1)
        def csfn(t, h=h):
            last = None
            for jb in range(8):
                for rb in range(8, 16):
                    last = t.matmul(
                        PS[:, 3584 + jb:3584 + jb + 1],
                        Ei[:, rb, 128 * jb:128 * (jb + 1)],
                        eMb[:, rb:rb + 1],
                        start=(rb == 8), stop=(rb == 15))
            return last
        A("T", csfn, 1, [("S", semb)])
        tcs = M("T")
        def vv2fn(v, h=h):
            last = None
            for rb in range(16):
                last = v.tensor_scalar_mul(vv2[:, rb, :],
                                           rn[:, rb, 128 * h:128 * (h + 1)],
                                           eMf[:, rb:rb + 1])
            return last
        A("V", vv2fn, 1, [("S", semf)])
        vvv2 = M("V")
        def y1fn(t, h=h):
            last = None
            for rb in range(16):
                for n in range(2):
                    last = t.matmul(PS[:, 512 * n:512 * (n + 1)],
                                    vv2[:, rb, :], Ei[:, rb, 512 * n:512 * (n + 1)],
                                    start=(rb == 0), stop=(rb == 15))
            return last
        A("T", y1fn, 1, [("V", vvv2), ("S", sexp_all)])
        ty1 = M("T")
        A("V", lambda v: v.tensor_copy(cs_sb[:, :], PS[:, 3584:3592]), 1,
          [("T", tcs), ("S", seng)])
        csV = M("V")
        A("V", lambda v: v.tensor_mul(cs_sb[:, :], cs_sb[:, :], eNg[:, 0:8]), 1)
        A("V", lambda v: v.tensor_add(Sc[:, 0:8], S1[:, 0:8], cs_sb[:, :]), 1)
        A("V", lambda v: v.tensor_add(Sc[:, 8:16], S1[:, 8:16], S2[:, 8:16]), 1)
        s1war = M("V")
        A("V", lambda v: v.reciprocal(rS[:, :], Sc[:, :]), 1)
        A("V", lambda v: v.tensor_mul(fcol[:, :], eNg[:, 0:8], rS[:, 0:8]), 1)
        A("V", lambda v: v.tensor_copy(fbf[:, :], fcol[:, :]), 1)
        vf = M("V")
        def vv1fn(v, h=h):
            last = None
            for rb in range(16):
                last = v.tensor_scalar_mul(vv1[:, rb, :],
                                           rn[:, rb, 128 * h:128 * (h + 1)],
                                           rS[:, rb:rb + 1])
            return last
        A("V", vv1fn, 1)
        vvv1 = M("V")
        def ftfn(t, h=h):
            last = None
            for jb in range(8):
                last = t.matmul(
                    PS[0:1, 2560 + 128 * jb:2560 + 128 * (jb + 1)],
                    fbf[:, jb:jb + 1], ident[:, :], start=True, stop=True)
            return last
        A("T", ftfn, 1, [("V", vf)])
        tft = M("T")
        A("V", lambda v: v.tensor_copy(frow[:, :], PS[0:1, 2560:3584]), 1, [("T", tft)])
        vfrow = M("V")
        def fbcfn(t, h=h):
            t.matmul(PS[:, 2560:3072], ones1[0:1, :], frow[0:1, 0:512],
                     start=True, stop=True)
            return t.matmul(PS[:, 3072:3584], ones1[0:1, :], frow[0:1, 512:1024],
                            start=True, stop=True)
        A("T", fbcfn, 1, [("V", vfrow)])
        tfbc = M("T")
        A("V", lambda v: v.tensor_copy(f_bc[:, :], PS[:, 2560:3584]), 1, [("T", tfbc)])
        prev_fbcV = M("V")
        A("V", lambda v: v.tensor_tensor(scY1[:, :], PS[:, 0:1024], f_bc[:, :],
                                         ALU.mult), 1, [("T", ty1)])
        prev_scY1 = M("V")
        def y2fn(t, h=h):
            last = None
            for rb in range(16):
                for n in range(2):
                    last = t.matmul(PS[:, 1024 + 512 * n:1024 + 512 * (n + 1)],
                                    vv1[:, rb, :], Ei[:, rb, 512 * n:512 * (n + 1)],
                                    start=(rb == 0), stop=(rb == 15))
            return last
        A("T", y2fn, 1, [("V", vvv1)])
        prev_y2T = M("T")
        A("V", lambda v: v.tensor_tensor(scY2[:, :], scY1[:, :], PS[:, 1024:2048],
                                         ALU.add), 1, [("T", prev_y2T)])
        prev_scY2 = M("V")
        A("V", lambda v, h=h: v.tensor_scalar_mul(y_st[:, h, :], scY2[:, :],
                                                  gv[:, 3:4]), 1)
        head_tail_V = M("V")
        head_tail_S = M("S")
        prev_csV = csV
        prev_exp15 = exp_done[15]
        prev_s1war = s1war
        if debug and h == 0:
            sch.barrier()
            A("Y", lambda y: y.dma_start(out=dbg["d_nM"].ap(), in_=nM[:, :]), 16, lane="Yi")
            A("Y", lambda y: y.dma_start(out=dbg["d_S"].ap(), in_=Sc[:, :]), 16, lane="Yi")
            A("Y", lambda y: y.dma_start(out=dbg["d_Ei"].ap(), in_=Ei[:, :, :]), 16, lane="Yi")
            A("Y", lambda y: y.dma_start(out=dbg["d_fbc"].ap(), in_=f_bc[:, :]), 16, lane="Yi")
            sch.barrier()

    bar = sch.barrier()
    if debug:
        A("Y", lambda y: y.dma_start(out=dbg["d_y"].ap(), in_=y_st[:, :, :]), 16, lane="Yi")
        sch.barrier()

    ws_marks = [None] * 16
    for kk in range(8):
        dst = ws_lo if kk < 4 else ws_hi
        ko = 2 * kk if kk < 4 else 2 * (kk - 4)
        ln = "Gw" if kk % 2 == 0 else "Gu"
        A("G", lambda g, kk=kk, dst=dst, ko=ko: g.dma_start(
            out=dst[:, ko:ko + 2, :],
            in_=ws_in[256 * kk:256 * (kk + 1), :].rearrange(
                "(k p) c -> p k c", k=2)), 16, lane=ln)
        ws_marks[2 * kk] = ws_marks[2 * kk + 1] = (ln, M(ln))
    sproj_user = [0, 0]
    s_stage_user = [0, 0]
    for tb in range(8):
        slot = tb % 2
        pofs = slot * 2048
        for fc in range(16):
            def spfn(t, tb=tb, pofs=pofs, fc=fc):
                lhsT = y_st[:, fc, 128 * tb:128 * (tb + 1)]
                rhs_t = ws_lo if fc < 8 else ws_hi
                last = None
                for n in range(4):
                    last = t.matmul(
                        PS[:, pofs + 512 * n:pofs + 512 * (n + 1)],
                        lhsT, rhs_t[:, fc % 8, 512 * n:512 * (n + 1)],
                        start=(fc == 0), stop=(fc == 15))
                return last
            kw = [("V", sproj_user[slot])] if fc == 0 else []
            A("T", spfn, 1, kw + [ws_marks[fc]])
        tsp = M("T")
        A("V", lambda v, slot=slot, pofs=pofs: v.tensor_copy(
            s_stage[slot][:, :], PS[:, pofs:pofs + 2048]), 1,
          [("T", tsp), ("Ys", s_stage_user[slot])])
        sproj_user[slot] = M("V")
        A("Y", lambda y, tb=tb, slot=slot: y.dma_start(
            out=s_dram[128 * tb:128 * (tb + 1), :], in_=s_stage[slot][:, :]), 16,
          [("V", sproj_user[slot])], lane="Ys")
        s_stage_user[slot] = M("Ys")

    bar = sch.barrier()
    if debug:
        A("Y", lambda y: y.dma_start(out=dbg["d_s"].ap(), in_=s_dram.ap()), 16, lane="Yi")
        sch.barrier()

    A("Y", lambda y: y.dma_start(out=sd_row[:, :], in_=sd_in.ap()), 16, lane="Yi")
    sdld = M("Yi")
    def sdfn(t):
        last = None
        for n in range(4):
            last = t.matmul(PS[:, 512 * n:512 * (n + 1)], ones1f[0:1, :],
                            sd_row[0:1, 512 * n:512 * (n + 1)],
                            start=True, stop=True)
        return last
    A("T", sdfn, 1, [("Yi", sdld)])
    tsd = M("T")
    A("V", lambda v: v.tensor_copy(sd_bc[:, :], PS[:, 0:2048]), 1, [("T", tsd)])
    vsd = M("V")
    if debug:
        A("Y", lambda y: y.dma_start(out=dbg["d_sdbc"].ap(), in_=sd_bc[:, :]), 16,
          [("V", vsd)], lane="Yi")

    wload = [0, 0]
    guv_userS = [0, 0, 0, 0]
    guv_userV = [0, 0, 0, 0]
    fuse_V2 = [0, 0]
    last_fuse_V = 0
    down_V_end = 0
    down_T_end = 0
    gate_T_end = 0
    guv_user = [0, 0, 0, 0]
    down_user = [0, 0]
    xr_user = [0, 0]
    outs_user = [0, 0]
    ssb_user = [0, 0]
    wd_user = [0, 0]
    out_store_marks = []

    for sp in range(2):
        for ic in range(16):
            wbi = ic % 2
            aw = ([("V", down_V_end), ("T", down_T_end)] if ic == 1 else [])
            A("G", lambda g, ic=ic, wbi=wbi: g.dma_start(
                out=wgb[wbi][:, :, :], in_=wg_in[ic, :, :, :]), 16,
              [("T", wload[wbi])] + aw, lane="Gg")
            A("G", lambda g, ic=ic, wbi=wbi: g.dma_start(
                out=wub[wbi][:, :, :], in_=wu_in[ic, :, :, :]), 16, aw, lane="Gu")
            gm_g = M("Gg"); gm_u = M("Gu")
            last_t = 0
            for ib in range(4):
                gb = ic * 4 + ib
                pg = (gb % 4) * 1024
                pu = pg + 512
                def gufn(t, sp=sp, ic=ic, ib=ib, wbi=wbi, pg=pg, pu=pu):
                    last = None
                    for k in range(16):
                        last = t.matmul(
                            PS[:, pg:pg + 512],
                            wgb[wbi][:, k, 128 * ib:128 * (ib + 1)],
                            xnT_lo[:, k, 512 * sp:512 * (sp + 1)],
                            start=(k == 0), stop=(k == 15))
                    for k in range(16):
                        last = t.matmul(
                            PS[:, pu:pu + 512],
                            wub[wbi][:, k, 128 * ib:128 * (ib + 1)],
                            xnT_lo[:, k, 512 * sp:512 * (sp + 1)],
                            start=(k == 0), stop=(k == 15))
                    return last
                extra = ([("V", down_V_end), ("V", vsd)] if ic == 0 else [])
                A("T", gufn, 1, [("Gg", gm_g), ("Gu", gm_u), ("S", guv_userS[gb % 4]),
                                 ("V", guv_userV[gb % 4])] + extra)
                tgu = M("T")
                last_t = tgu
                col = ic * 4 + ib
                gp = gb % 2
                A("S", lambda s, col=col, pg=pg, gp=gp: s.activation(
                    guscr[gp][:, :], PS[:, pg:pg + 512], AF.Silu,
                    bias=0.0, scale=sgv[:, col:col + 1]), 1,
                  [("T", tgu), ("V", fuse_V2[gp])])
                guv_userS[gb % 4] = M("S")
                A("V", lambda v, col=col, pu=pu, gp=gp: v.tensor_scalar_mul(
                    guscr2[gp][:, :], PS[:, pu:pu + 512], suv[:, col:col + 1]),
                  1, [("T", tgu), ("S", guv_userS[gb % 4])])
                A("V", lambda v, col=col, gp=gp: v.tensor_tensor(
                    fuse[:, col, :], guscr[gp][:, :], guscr2[gp][:, :], ALU.mult), 1)
                guv_userV[gb % 4] = M("V")
                fuse_V2[gp] = M("V")
                last_fuse_V = M("V")
            wload[wbi] = last_t
        fuse_done_V = M("V")
        gate_T_end = M("T")
        if debug and sp == 0:
            sch.barrier()
            A("Y", lambda y: y.dma_start(out=dbg["d_fuse"].ap(), in_=fuse[:, :, :]), 16, lane="Yi")
            sch.barrier()
        for tp in range(2):
            for tb2 in range(2):
                tbg = sp * 4 + tp * 2 + tb2
                A("Y", lambda y, tbg=tbg, b=tb2: y.dma_start(
                    out=xr_t[b][:, :], in_=x_in[128 * tbg:128 * (tbg + 1), :]), 16,
                  [("V", xr_user[tb2]), ("V", fuse_done_V)], lane="Yx")
                A("Y", lambda y, tbg=tbg, b=tb2: y.dma_start(
                    out=s_sb[b][:, :], in_=s_dram[128 * tbg:128 * (tbg + 1), :]), 16,
                  [("V", ssb_user[tb2]), ("T", gate_T_end)], lane="Ys")
            yxr = M("Yx"); ysb = M("Ys")
            dn_t = 0
            for r in range(32):
                wbi = r % 2
                A("G", lambda g, r=r, wbi=wbi: g.dma_start(
                    out=wdb[wbi][:, :, :],
                    in_=wd_in[256 * r:256 * (r + 1), :].rearrange(
                        "(u p) c -> p u c", u=2)), 16,
                  [("T", wd_user[wbi]), ("T", gate_T_end)], lane="Gd")
                gw = M("Gd")
                def dnfn(t, sp=sp, tp=tp, r=r, wbi=wbi):
                    last = None
                    for u in range(2):
                        ic = 2 * r + u
                        for tb2 in range(2):
                            pofs = tb2 * 2048
                            for n in range(4):
                                last = t.matmul(
                                    PS[:, pofs + 512 * n:pofs + 512 * (n + 1)],
                                    fuse[:, ic, 256 * tp + 128 * tb2:256 * tp + 128 * (tb2 + 1)],
                                    wdb[wbi][:, u, 512 * n:512 * (n + 1)],
                                    start=(r == 0 and u == 0), stop=(r == 31 and u == 1))
                    return last
                A("T", dnfn, 1, [("Gd", gw), ("V", fuse_done_V)] +
                  ([("V", down_user[0]), ("V", down_user[1])] if r == 0 else []))
                dn_t = M("T")
                wd_user[wbi] = dn_t
            for tb2 in range(2):
                tbg = sp * 4 + tp * 2 + tb2
                pofs = tb2 * 2048
                A("V", lambda v, pofs=pofs: v.tensor_tensor(
                    scrA[:, :], PS[:, pofs:pofs + 2048], sd_bc[:, :], ALU.mult),
                  1, [("T", dn_t), ("V", vsd), ("Yx", yxr), ("Ys", ysb)])
                down_user[tb2] = M("V")
                if debug and sp == 0 and tp == 0 and tb2 == 0:
                    A("Y", lambda y: y.dma_start(out=dbg["d_dn"].ap(), in_=scrA[:, :]),
                      16, [("V", down_user[0])], lane="Yi")
                    dnst = M("Yi")
                    sch.ops["S"].append((None, 1, (("Yi", dnst),), "S"))
                    sch.cnt["S"] += 1
                A("S", lambda s, b=tb2: s.activation(outs_t[b][:, :], scrA[:, :],
                                                     AF.Silu, bias=0.0, scale=1.0), 1,
                  [("V", down_user[tb2]), ("Yo", outs_user[tb2])])
                sg_m = M("S")
                A("V", lambda v, b=tb2: v.tensor_tensor(
                    scrA[:, :], outs_t[b][:, :], s_sb[b][:, :], ALU.mult), 1,
                  [("S", sg_m)])
                ssb_user[tb2] = M("V")
                A("V", lambda v, b=tb2: v.tensor_tensor(
                    outs_t[b][:, :], scrA[:, :], xr_t[b][:, :], ALU.add), 1)
                xr_user[tb2] = M("V")
                vout = M("V")
                A("Y", lambda y, tbg=tbg, b=tb2: y.dma_start(
                    out=out_hbm[128 * tbg:128 * (tbg + 1), :], in_=outs_t[b][:, :]),
                  16, [("V", vout)], lane="Yo")
                outs_user[tb2] = M("Yo")
                out_store_marks.append(M("Yo"))
            down_V_end = M("V")
            down_T_end = M("T")

    A("Y", lambda y: y.nop(), 1,
      [(ln, M(ln)) for ln in Sched.LANES], lane="y")

    from contextlib import ExitStack
    with ExitStack() as ctx:
        e = ctx.enter_context
        global PS
        psg = e(nc.psum_tensor("PS", [128, 4096], F32))
        PS_set(psg)
        sems = {ln: e(nc.semaphore("sem" + ln + ("e" if ln.islower() else "")))
                for ln in Sched.LANES}
        block = e(nc.Block())

        def emit(eng_name, eng):
            last_wait = {k: 0 for k in Sched.LANES}
            done = {k: 0 for k in Sched.LANES}
            for (fn, inc, waits, lane) in sch.ops[eng_name]:
                ws = list(waits)
                if lane != "T" and done[lane] > last_wait[lane]:
                    ws.append((lane, done[lane]))
                for (we, wv) in ws:
                    if wv > last_wait[we]:
                        eng.wait_ge(sems[we], wv)
                        last_wait[we] = wv
                if fn is None:
                    inst = eng.nop()
                else:
                    inst = fn(eng)
                if inc:
                    inst.then_inc(sems[lane], inc)
                done[lane] += inc

        block.tensor(lambda t: emit("T", t))
        block.vector(lambda v: emit("V", v))
        block.scalar(lambda s: emit("S", s))
        block.sync(lambda y: emit("Y", y))
        block.gpsimd(lambda g: emit("G", g))
    return nc


PS = None


def PS_set(handle):
    global PS
    PS = handle


_NC_CACHE = {}


def _get_nc():
    if "nc" not in _NC_CACHE:
        _NC_CACHE["nc"] = build()
    return _NC_CACHE["nc"]


def kernel(L0_attn_x, L0_attn_W, L0_attn_g1, L0_attn_g2, L0_attn_g3,
           L0_ffn_w_gate, L0_ffn_w_up, L0_ffn_w_down,
           L0_ffn_s_gate, L0_ffn_s_up, L0_ffn_s_down):
    x = np.ascontiguousarray(np.asarray(L0_attn_x, dtype=np.float32))
    W = np.ascontiguousarray(np.asarray(L0_attn_W, dtype=np.float32))
    g1 = np.asarray(L0_attn_g1, np.float32).reshape(128, 1)
    g2 = np.asarray(L0_attn_g2, np.float32).reshape(128, 1)
    g3 = np.asarray(L0_attn_g3, np.float32).reshape(128, 1)
    wg = np.asarray(L0_ffn_w_gate, np.int8)
    wu = np.asarray(L0_ffn_w_up, np.int8)
    wd = np.ascontiguousarray(np.asarray(L0_ffn_w_down, np.int8))
    sg = np.asarray(L0_ffn_s_gate, np.float32)
    su = np.asarray(L0_ffn_s_up, np.float32)
    sd = np.asarray(L0_ffn_s_down, np.float32).reshape(1, 2048)

    wt = np.ascontiguousarray(W.T)
    wg4 = np.ascontiguousarray(wg.reshape(16, 128, 16, 512).transpose(2, 1, 0, 3))
    wu4 = np.ascontiguousarray(wu.reshape(16, 128, 16, 512).transpose(2, 1, 0, 3))
    sgv = np.ascontiguousarray(sg.reshape(64, 128).T)
    suv = np.ascontiguousarray(su.reshape(64, 128).T)
    ident = np.eye(128).astype(ml_dtypes.bfloat16)

    common = {
        "wt_in": wt, "ws_in": W, "g1_in": g1, "g2_in": g2, "g3_in": g3,
        "wg_in": wg4, "wu_in": wu4, "wd_in": wd, "sg_in": sgv, "su_in": suv,
        "sd_in": sd, "id_in": ident,
    }
    in_maps = []
    for c in range(8):
        b, g = c // 2, c % 2
        if g == 0:
            xc = x[b]
        else:
            xc = np.ascontiguousarray(np.concatenate([x[b][1024:], x[b][:1024]], axis=0))
        in_maps.append(dict(common, x_in=xc))

    nc = _get_nc()
    res = run_bass_kernel_spmd(nc, in_maps, core_ids=list(range(8)))
    out = np.empty((4, 2048, 2048), np.float32)
    for c in range(8):
        b, g = c // 2, c % 2
        out[b, 1024 * g:1024 * (g + 1), :] = res.results[c]["out"]
    return out
